# version 01
# baseline (speedup 1.0000x reference)
import numpy as np
import ml_dtypes
from contextlib import ExitStack

import concourse.bass as bass
import concourse.mybir as mybir
from concourse.bass_utils import run_bass_kernel_spmd

F32 = mybir.dt.float32
BF16 = mybir.dt.bfloat16
AF = mybir.ActivationFunctionType
ALU = mybir.AluOpType
BF = ml_dtypes.bfloat16

NCORES = 8
D = 2048
B = 2
T = 4096
NTOK = B * T
TSL = NTOK // NCORES
NH = 16
DH = 128
HPC = NH // NCORES
CH = HPC * DH
TG = 512
NTG = NTOK // TG
KC = D // 128
SCALE_K = float(DH) ** -0.5
LN_EPS = 1e-5


class Chain:

    def __init__(self, sem):
        self.sem = sem
        self.n = 0

    def wait(self, eng):
        if self.n:
            eng.wait_ge(self.sem, self.n)

    def bump(self, instr):
        instr.then_inc(self.sem, 1)
        self.n += 1

    def bump_d(self, eng):
        eng.drain()
        eng.nop().then_inc(self.sem, 1)
        self.n += 1


class DmaCtr:
    def __init__(self, sem):
        self.sem = sem
        self.n = 0

    def start(self, eng, out, in_):
        eng.dma_start(out=out, in_=in_).then_inc(self.sem, 16)
        self.n += 16

    def wait_all(self, eng):
        if self.n:
            eng.wait_ge(self.sem, self.n)


def slab_in(dram_ap, tg=None):
    v = dram_ap.rearrange("(k p) t -> p k t", p=128)
    if tg is not None:
        v = v[:, :, tg * TG:(tg + 1) * TG]
    return v



DEBUG_NTG = [NTG]
DEBUG_ALIGN = [0]
DEBUG_PH = [99]


def build_launch1():
    nc = bass.Bass("TRN2", target_bir_lowering=False)
    xT = nc.dram_tensor("xT", [D, NTOK], F32, kind="ExternalInput").ap()
    PROJ = ["q", "k", "v", "a", "g"]
    w_in = {p: nc.dram_tensor(f"w{p}T", [D, CH], F32, kind="ExternalInput").ap()
            for p in PROJ}
    w_in["b"] = nc.dram_tensor("wbT", [D, HPC], F32, kind="ExternalInput").ap()
    ba = nc.dram_tensor("ba", [128, HPC], F32, kind="ExternalInput").ap()
    bb = nc.dram_tensor("bb", [HPC, 1], F32, kind="ExternalInput").ap()
    convw = {p: nc.dram_tensor(f"{p}cw", [128, HPC * 4], F32, kind="ExternalInput").ap()
             for p in ("q", "k", "v")}
    convb = {p: nc.dram_tensor(f"{p}cb", [128, HPC], F32, kind="ExternalInput").ap()
             for p in ("q", "k", "v")}
    onescol_bf = nc.dram_tensor("onescol_bf", [128, 1], BF16, kind="ExternalInput").ap()
    onesrow_f = nc.dram_tensor("onesrow_f", [1, 128], BF16, kind="ExternalInput").ap()

    outs = {}
    for nm, rows_, dt in [("q", CH, BF16), ("k", CH, BF16), ("v", CH, BF16),
                          ("a", CH, BF16), ("g", CH, BF16), ("b", HPC, F32)]:
        outs[nm] = nc.dram_tensor(f"{nm}To", [rows_, NTOK], dt,
                                  kind="ExternalOutput").ap()
    xnTo = nc.dram_tensor("xnTo", [D, NTOK], BF16, kind="ExternalOutput").ap()

    with ExitStack() as ctx:
        E = ctx.enter_context
        wbf = {p: E(nc.sbuf_tensor(f"w_{p}", [128, KC * CH], BF16)) for p in PROJ}
        wbbf = E(nc.sbuf_tensor("w_b", [128, KC * HPC], BF16))
        ba_t = E(nc.sbuf_tensor("ba_t", [128, HPC], F32))
        bb_t = E(nc.sbuf_tensor("bb_t", [HPC, 1], F32))
        cw = {p: E(nc.sbuf_tensor(f"{p}cw_t", [128, HPC * 4], F32))
              for p in ("q", "k", "v")}
        cb = {p: E(nc.sbuf_tensor(f"{p}cb_t", [128, HPC], F32))
              for p in ("q", "k", "v")}
        ones_c = E(nc.sbuf_tensor("ones_c", [128, 1], BF16))
        ones_r = E(nc.sbuf_tensor("ones_r", [1, 128], BF16))
        xbf = E(nc.sbuf_tensor("xbf", [128, KC * TG], BF16))
        xsq = E(nc.sbuf_tensor("xsq", [128, KC * TG], BF16))
        rowsb = E(nc.sbuf_tensor("rowsb", [1, 8 * TG], F32))
        rstd_bf = E(nc.sbuf_tensor("rstd_bf", [1, TG], BF16))
        mrs_bf = E(nc.sbuf_tensor("mrs_bf", [1, TG], BF16))
        rstdbc = E(nc.sbuf_tensor("rstdbc", [128, TG], F32))
        mrsbc = E(nc.sbuf_tensor("mrsbc", [128, TG], F32))
        xn = E(nc.sbuf_tensor("xn", [128, KC * TG], BF16))
        raw = {p: [E(nc.sbuf_tensor(f"raw_{p}{j}", [128, TG + 3], F32))
                   for j in range(HPC)] for p in ("q", "k", "v")}
        accv = {p: [E(nc.sbuf_tensor(f"acc_{p}{j}", [128, TG], F32))
                    for j in range(HPC)] for p in ("q", "k", "v")}
        stage = {p: [E(nc.sbuf_tensor(f"st_{p}{j}", [128, TG], BF16))
                     for j in range(HPC)] for p in ("q", "k", "v", "a", "g")}
        kfin = [E(nc.sbuf_tensor(f"kfin{j}", [128, TG], BF16)) for j in range(HPC)]
        stageb = E(nc.sbuf_tensor("stageb", [HPC, TG], F32))

        ps = [E(nc.psum_tensor(f"ps{i}", [128, TG], F32)) for i in range(8)]

        ld = E(nc.semaphore("ld"))
        ch_sem = E(nc.semaphore("ch"))
        st = E(nc.semaphore("st"))
        block = E(nc.Block())

        dma = DmaCtr(ld)
        sto = DmaCtr(st)
        chain = Chain(ch_sem)

        def g_loads(gp):
            for p in PROJ:
                dma.start(gp, wbf[p][:].rearrange("p (k c) -> p k c", k=KC),
                          w_in[p].rearrange("(k p) c -> p k c", p=128))
            dma.start(gp, wbbf[:].rearrange("p (k c) -> p k c", k=KC),
                      w_in["b"].rearrange("(k p) c -> p k c", p=128))
        block.gpsimd(g_loads)

        def s_loads(sync):
            dma.start(sync, ba_t[:], ba)
            dma.start(sync, bb_t[:], bb)
            for p in ("q", "k", "v"):
                dma.start(sync, cw[p][:], convw[p])
                dma.start(sync, cb[p][:], convb[p])
            dma.start(sync, ones_c[:], onescol_bf)
            dma.start(sync, ones_r[:], onesrow_f)
        block.sync(s_loads)

        R = {"mean": 0, "msq": 1, "sq": 2, "var": 3, "lnv": 4, "rstd": 5, "mrs": 6}

        def row(nm):
            i = R[nm]
            return rowsb[:, i * TG:(i + 1) * TG]

        PHC = [0]
        for ig in range(DEBUG_NTG[0]):
            def ph(ig=ig):
                PHC[0] += 1
                return (ig > 0) or (PHC[0] <= DEBUG_PH[0])

            def g_x(gp, ig=ig):
                chain.wait(gp)
                dma.start(gp, xbf[:].rearrange("p (k t) -> p k t", k=KC),
                          slab_in(xT, ig))
            ph() and block.gpsimd(g_x)

            def s_sq(s, ig=ig):
                dma.wait_all(s)
                for k in range(KC):
                    s.activation(xsq[:, k * TG:(k + 1) * TG],
                                 xbf[:, k * TG:(k + 1) * TG], AF.Square)
                chain.bump_d(s)
            ph() and block.scalar(s_sq)

            def t_stat(t, ig=ig):
                chain.wait(t)
                for k in range(KC):
                    t.matmul(ps[5][0:1, :], ones_c[:], xbf[:, k * TG:(k + 1) * TG],
                             start=(k == 0), stop=(k == KC - 1))
                for k in range(KC):
                    t.matmul(ps[5][32:33, :], ones_c[:], xsq[:, k * TG:(k + 1) * TG],
                             start=(k == 0), stop=(k == KC - 1))
                chain.bump_d(t)
            ph() and block.tensor(t_stat)

            def s_rows(s, ig=ig):
                chain.wait(s)
                s.activation(row("mean"), ps[5][0:1, :], AF.Copy, scale=1.0 / D)
                s.activation(row("msq"), ps[5][32:33, :], AF.Copy, scale=1.0 / D)
                chain.bump_d(s)
            ph() and block.scalar(s_rows)

            def v_rows(v, ig=ig):
                chain.wait(v)
                v.tensor_tensor(row("sq"), row("mean"), row("mean"), ALU.mult)
                v.drain()
                v.tensor_scalar(row("sq"), row("sq"), 1.0, -LN_EPS, ALU.mult,
                                ALU.add)
                v.drain()
                v.tensor_tensor(row("var"), row("msq"), row("sq"), ALU.subtract)
                chain.bump_d(v)
            ph() and block.vector(v_rows)

            def s_rstd(s, ig=ig):
                chain.wait(s)
                s.activation(row("lnv"), row("var"), AF.Ln)
                s.drain()
                s.activation(row("rstd"), row("lnv"), AF.Exp, scale=-0.5)
                chain.bump_d(s)
            ph() and block.scalar(s_rstd)

            def v_mrs(v, ig=ig):
                chain.wait(v)
                v.tensor_tensor(row("mrs"), row("mean"), row("rstd"), ALU.mult)
                v.tensor_copy(rstd_bf[:], row("rstd"))
                v.drain()
                v.tensor_copy(mrs_bf[:], row("mrs"))
                chain.bump_d(v)
            ph() and block.vector(v_mrs)

            def t_bc(t, ig=ig):
                chain.wait(t)
                t.matmul(ps[4][:, :], ones_r[:], rstd_bf[:], start=True, stop=True)
                t.matmul(ps[7][:, :], ones_r[:], mrs_bf[:], start=True, stop=True)
                chain.bump_d(t)
            ph() and block.tensor(t_bc)

            def s_bc_evac(s, ig=ig):
                chain.wait(s)
                s.copy(rstdbc[:], ps[4][:, :])
                s.copy(mrsbc[:], ps[7][:, :])
                chain.bump_d(s)
            ph() and block.scalar(s_bc_evac)

            def v_xn(v, ig=ig):
                chain.wait(v)
                for k in range(8):
                    v.tensor_tensor(xn[:, k * TG:(k + 1) * TG],
                                    xbf[:, k * TG:(k + 1) * TG], rstdbc[:],
                                    ALU.mult)
                for k in range(8):
                    v.tensor_tensor(xn[:, k * TG:(k + 1) * TG],
                                    xn[:, k * TG:(k + 1) * TG], mrsbc[:],
                                    ALU.subtract)
                chain.bump_d(v)
            ph() and block.vector(v_xn)

            def p_xn(gp, ig=ig):
                chain.wait(gp)
                for k in range(8, KC):
                    gp.tensor_tensor(xn[:, k * TG:(k + 1) * TG],
                                     xbf[:, k * TG:(k + 1) * TG], rstdbc[:],
                                     ALU.mult)
                for k in range(8, KC):
                    gp.tensor_tensor(xn[:, k * TG:(k + 1) * TG],
                                     xn[:, k * TG:(k + 1) * TG], mrsbc[:],
                                     ALU.subtract)
                chain.bump_d(gp)
            ph() and block.gpsimd(p_xn)

            def g_xn_out(gp, ig=ig):
                chain.wait(gp)
                sto.start(gp, slab_in(xnTo, ig),
                          xn[:].rearrange("p (k t) -> p k t", k=KC))
            ph() and block.gpsimd(g_xn_out)

            def mk_proj_mm(p, b0, ig=ig):
                def t_p(t):
                    chain.wait(t)
                    for j in range(HPC):
                        bank = ps[b0 + j]
                        for k in range(KC):
                            t.matmul(bank[:, :],
                                     wbf[p][:, k * CH + j * 128:k * CH + (j + 1) * 128],
                                     xn[:, k * TG:(k + 1) * TG],
                                     start=(k == 0), stop=(k == KC - 1))
                    chain.bump_d(t)
                return t_p

            def mk_qkv_hist(p, ig=ig):
                def v_h(v):
                    chain.wait(v)
                    for j in range(HPC):
                        if ig % 8 == 0:
                            v.memset(raw[p][j][:, 0:3], 0.0)
                        else:
                            v.tensor_copy(raw[p][j][:, 0:3],
                                          raw[p][j][:, TG:TG + 3])
                    chain.bump_d(v)
                return v_h

            def mk_qkv_evac(p, b0, ig=ig):
                def s_e(s):
                    chain.wait(s)
                    for j in range(HPC):
                        s.copy(raw[p][j][:, 3:TG + 3], ps[b0 + j][:, :])
                    chain.bump_d(s)
                return s_e

            def mk_sig(p, b0, ig=ig):
                def s_e(s):
                    chain.wait(s)
                    for j in range(HPC):
                        if p == "a":
                            s.activation(stage[p][j][:], ps[b0 + j][:, :],
                                         AF.Sigmoid, bias=ba_t[:, j:j + 1])
                        else:
                            s.activation(stage[p][j][:], ps[b0 + j][:, :],
                                         AF.Sigmoid)
                    chain.bump_d(s)
                return s_e

            for pi, p in enumerate(PROJ):
                b0 = (pi % 2) * 2
                ph() and block.tensor(mk_proj_mm(p, b0))
                if p in ("q", "k", "v"):
                    ph() and block.vector(mk_qkv_hist(p))
                    ph() and block.scalar(mk_qkv_evac(p, b0))
                else:
                    ph() and block.scalar(mk_sig(p, b0))

            def t_beta(t, ig=ig):
                chain.wait(t)
                for k in range(KC):
                    t.matmul(ps[6][0:HPC, :], wbbf[:, k * HPC:(k + 1) * HPC],
                             xn[:, k * TG:(k + 1) * TG],
                             start=(k == 0), stop=(k == KC - 1))
                chain.bump_d(t)
            ph() and block.tensor(t_beta)

            def s_beta(s, ig=ig):
                chain.wait(s)
                s.activation(stageb[:], ps[6][0:HPC, :], AF.Sigmoid,
                             bias=bb_t[:, 0:1])
                chain.bump_d(s)
            ph() and block.scalar(s_beta)

            def v_conv(v, ig=ig):
                chain.wait(v)
                for tap in range(4):
                    for p in ("q", "k", "v"):
                        for j in range(HPC):
                            rw = raw[p][j][:, tap:tap + TG]
                            wcol = cw[p][:, j * 4 + tap:j * 4 + tap + 1]
                            if tap == 0:
                                v.tensor_scalar(accv[p][j][:], rw, wcol,
                                                cb[p][:, j:j + 1], ALU.mult,
                                                ALU.add)
                            else:
                                v.scalar_tensor_tensor(accv[p][j][:], rw, wcol,
                                                       accv[p][j][:], ALU.mult,
                                                       ALU.add)
                chain.bump_d(v)
            ph() and block.vector(v_conv)

            def s_silu(s, ig=ig):
                chain.wait(s)
                for p in ("q", "k", "v"):
                    for j in range(HPC):
                        s.activation(stage[p][j][:], accv[p][j][:], AF.Silu)
                chain.bump_d(s)
            ph() and block.scalar(s_silu)

            def v_ks(v, ig=ig):
                chain.wait(v)
                for j in range(HPC):
                    v.tensor_scalar(kfin[j][:], stage["k"][j][:], SCALE_K, None,
                                    ALU.mult)
                chain.bump_d(v)
            ph() and block.vector(v_ks)

            def g_out(gp, ig=ig):
                chain.wait(gp)
                for p in ("q", "v", "a", "g"):
                    for j in range(HPC):
                        sto.start(gp, outs[p][j * 128:(j + 1) * 128,
                                               ig * TG:(ig + 1) * TG],
                                  stage[p][j][:])
                for j in range(HPC):
                    sto.start(gp, outs["k"][j * 128:(j + 1) * 128,
                                            ig * TG:(ig + 1) * TG], kfin[j][:])
                sto.start(gp, outs["b"][:, ig * TG:(ig + 1) * TG], stageb[:])
                chain.bump_d(gp)
            ph() and block.gpsimd(g_out)

        def fin(sync):
            chain.wait(sync)
            sto.wait_all(sync)
        block.sync(fin)

    return nc


C2C = 64
NCHK = T // C2C
BLK = 8


def build_launch2():
    nc = bass.Bass("TRN2", target_bir_lowering=False)
    ins = {}
    for nm, rows_, dt in [("q", CH, BF16), ("k", CH, BF16), ("v", CH, BF16),
                          ("a", CH, BF16), ("g", CH, BF16), ("b", HPC, F32)]:
        ins[nm] = nc.dram_tensor(f"{nm}Ti", [rows_, NTOK], dt,
                                 kind="ExternalInput").ap()
    lnw = nc.dram_tensor("lnw", [128, 1], F32, kind="ExternalInput").ap()
    lnb = nc.dram_tensor("lnb", [128, 1], F32, kind="ExternalInput").ap()
    ident = nc.dram_tensor("ident", [128, 128], BF16, kind="ExternalInput").ap()
    onescol_bf = nc.dram_tensor("onescol_bf", [128, 1], BF16,
                                kind="ExternalInput").ap()
    triu = nc.dram_tensor("triu", [C2C, C2C], BF16, kind="ExternalInput").ap()
    mflat = nc.dram_tensor("mflat", [4, C2C * C2C], BF16, kind="Internal").ap()
    orow = nc.dram_tensor("orow", [4, 2 * TG], F32, kind="Internal").ap()
    oraw = nc.dram_tensor("oraw", [CH, NTOK], BF16, kind="Internal").ap()
    oTo = nc.dram_tensor("oTo", [CH, NTOK], BF16, kind="ExternalOutput").ap()

    with ExitStack() as ctx:
        E = ctx.enter_context
        tin = {nm: [E(nc.sbuf_tensor(f"in_{nm}{s}", [128, BLK * C2C], BF16))
                    for s in range(4)] for nm in ("q", "k", "v", "a")}
        brow = [E(nc.sbuf_tensor(f"brow{s}", [1, BLK * C2C], F32)) for s in range(4)]
        lnw_t = E(nc.sbuf_tensor("lnw_t", [128, 1], F32))
        lnb_t = E(nc.sbuf_tensor("lnb_t", [128, 1], F32))
        ident_t = E(nc.sbuf_tensor("ident_t", [128, 128], BF16))
        ones_c = E(nc.sbuf_tensor("ones_c", [128, 1], BF16))
        triu_t = E(nc.sbuf_tensor("triu_t", [C2C, C2C], BF16))
        ones64 = E(nc.sbuf_tensor("ones64", [128, C2C], F32))
        def quad(nm, w=C2C, dt=F32, p=128):
            return [E(nc.sbuf_tensor(f"{nm}{s}", [p, w], dt)) for s in range(4)]
        la = quad("la")
        cum = quad("cum")
        cumn = quad("cumn")
        Gp = quad("Gp")
        iGp = quad("iGp")
        Graw = quad("Graw")
        inva = quad("inva")
        nh = quad("nh")
        bbc = quad("bbc")
        t1q = quad("t1q")
        c1q = quad("c1q")
        bgq = quad("bgq")
        t3q = quad("t3q")
        RT = quad("RT", 2 * C2C)
        scr = quad("scr", 2 * C2C)
        Mbc = quad("Mbc", C2C * C2C, BF16)
        Msb = quad("Msb", C2C, BF16, p=C2C)
        Wbf = quad("Wbf", C2C, BF16)
        KQm = quad("KQm", C2C, BF16, p=C2C)
        Ktime = quad("Ktime", 128, BF16, p=C2C)
        Wtime = quad("Wtime", 128, BF16, p=C2C)
        Sst = quad("Sst", 128, BF16)
        SstT = quad("SstT", 128, BF16)
        t4q = quad("t4q", 128)
        oblk = quad("oblk", BLK * C2C, BF16)
        osq = E(nc.sbuf_tensor("osq", [128, TG], BF16))
        prow = E(nc.sbuf_tensor("prow", [1, 8 * TG], F32))
        rstdbc2 = E(nc.sbuf_tensor("rstdbc2", [128, TG], F32))
        mrsbc2 = E(nc.sbuf_tensor("mrsbc2", [128, TG], F32))
        gblk = E(nc.sbuf_tensor("gblk", [128, TG], BF16))
        oout = E(nc.sbuf_tensor("oout", [128, TG], BF16))
        ot1 = E(nc.sbuf_tensor("ot1", [128, TG], F32))

        ps_bp = E(nc.psum_tensor("ps_bp", [128, 512], F32))
        ps_mkq = E(nc.psum_tensor("ps_mkq", [128, 512], F32))
        ps_wkq = E(nc.psum_tensor("ps_wkq", [128, 256], F32))
        ps_u = E(nc.psum_tensor("ps_u", [128, 512], F32))
        ps_t1 = E(nc.psum_tensor("ps_t1", [128, 1024], BF16))
        ps_t2 = E(nc.psum_tensor("ps_t2", [128, 512], BF16))
        ps_post = E(nc.psum_tensor("ps_post", [128, 512], F32))

        ld = E(nc.semaphore("ld"))
        ch_sem = E(nc.semaphore("ch"))
        st = E(nc.semaphore("st"))
        block = E(nc.Block())

        dma = DmaCtr(ld)
        sto = DmaCtr(st)
        chain = Chain(ch_sem)

        def s_loads(sync):
            dma.start(sync, lnw_t[:], lnw)
            dma.start(sync, lnb_t[:], lnb)
            dma.start(sync, ident_t[:], ident)
            dma.start(sync, ones_c[:], onescol_bf)
            dma.start(sync, triu_t[:], triu)
        block.sync(s_loads)

        def v_init(v):
            dma.wait_all(v)
            v.memset(ones64[:], 1.0)
            for s in range(4):
                v.memset(Sst[s][:], 0.0)
                v.memset(SstT[s][:], 0.0)
            chain.bump_d(v)
        block.vector(v_init)

        def srcsl(s, ci, w=C2C):
            j, b = s % HPC, s // HPC
            off = b * T + ci * C2C
            return (slice(j * 128, (j + 1) * 128), slice(off, off + w))

        for ci in range(NCHK):
            bi = ci % BLK
            if bi == 0:
                def g_blk(gp, ci=ci):
                    chain.wait(gp)
                    for s in range(4):
                        rs, cs = srcsl(s, ci, BLK * C2C)
                        for nm in ("q", "k", "v", "a"):
                            dma.start(gp, tin[nm][s][:], ins[nm][rs, cs])

                block.gpsimd(g_blk)

            csl = slice(bi * C2C, (bi + 1) * C2C)

            def s_ln(s_, ci=ci, csl=csl):
                dma.wait_all(s_)
                chain.wait(s_)
                for s in range(4):
                    s_.activation(la[s][:], tin["a"][s][:, csl], AF.Ln)
                s_.drain()
                for s in range(4):
                    s_.activation(inva[s][:], la[s][:], AF.Exp, scale=-1.0)
                chain.bump_d(s_)
            block.scalar(s_ln)

            def g_bbc(gp, ci=ci):
                chain.wait(gp)
                for s in range(4):
                    j, b = s % HPC, s // HPC
                    off = b * T + ci * C2C
                    dma.start(gp, bbc[s][:],
                              ins["b"][j:j + 1, off:off + C2C]
                              .broadcast_to([128, C2C]))
            block.gpsimd(g_bbc)

            def v_cum(v, ci=ci):
                chain.wait(v)
                for s in range(4):
                    v.tensor_tensor_scan(cum[s][:], ones64[:], la[s][:], 0.0,
                                         ALU.mult, ALU.add)
                for s in range(4):
                    v.tensor_scalar(cumn[s][:], cum[s][:],
                                    cum[s][:, C2C // 2 - 1:C2C // 2], None,
                                    ALU.subtract)
                chain.bump_d(v)
            block.vector(v_cum)

            def s_exp(s_, ci=ci):
                chain.wait(s_)
                for s in range(4):
                    s_.activation(Gp[s][:], cumn[s][:], AF.Exp)
                for s in range(4):
                    s_.activation(iGp[s][:], cumn[s][:], AF.Exp, scale=-1.0)
                for s in range(4):
                    s_.activation(Graw[s][:], cum[s][:], AF.Exp)
                chain.bump_d(s_)
            block.scalar(s_exp)

            def t_mm1(t, ci=ci, csl=csl):
                chain.wait(t)
                for s in range(4):
                    t.matmul(ps_bp[:, s * 128:s * 128 + 64], SstT[s][:],
                             tin["k"][s][:, csl], start=True, stop=True)
                    t.matmul(ps_bp[:, s * 128 + 64:(s + 1) * 128], SstT[s][:],
                             tin["q"][s][:, csl], start=True, stop=True)
                    t.matmul(ps_mkq[0:64, s * 64:(s + 1) * 64],
                             tin["k"][s][:, csl], tin["k"][s][:, csl],
                             start=True, stop=True)
                    t.matmul(ps_mkq[0:64, 256 + s * 64:256 + (s + 1) * 64],
                             tin["k"][s][:, csl], tin["q"][s][:, csl],
                             start=True, stop=True)
                    t.transpose(ps_t1[0:64, s * 128:(s + 1) * 128],
                                tin["k"][s][:, csl], ident_t[:])
                chain.bump_d(t)
            block.tensor(t_mm1)

            def s_ev1(s_, ci=ci):
                chain.wait(s_)
                for s in range(4):
                    s_.copy(Msb[s][:], ps_mkq[0:64, s * 64:(s + 1) * 64])
                    s_.copy(Ktime[s][:], ps_t1[0:64, s * 128:(s + 1) * 128])
                chain.bump_d(s_)
            block.scalar(s_ev1)

            def g_mflat(gp, ci=ci):
                chain.wait(gp)
                m_ld = DmaCtr(ld)
                for s in range(4):
                    dma.start(gp, mflat[s:s + 1, :].rearrange(
                        "one (p n) -> (one p) n", p=C2C), Msb[s][:])
                gp.wait_ge(ld, dma.n)
                for s in range(4):
                    dma.start(gp, Mbc[s][:],
                              mflat[s:s + 1, :].broadcast_to([128, C2C * C2C]))
            block.gpsimd(g_mflat)

            def v_c2(v, ci=ci, csl=csl):
                dma.wait_all(v)
                chain.wait(v)
                for s in range(4):
                    v.tensor_tensor(t1q[s][:], tin["v"][s][:, csl], iGp[s][:],
                                    ALU.mult)
                for s in range(4):
                    v.tensor_tensor(c1q[s][:], t1q[s][:], bbc[s][:], ALU.mult)
                for s in range(4):
                    v.tensor_scalar(bgq[s][:], ps_bp[:, s * 128:s * 128 + 64],
                                    Graw[s][:, C2C // 2 - 1:C2C // 2], None,
                                    ALU.mult)
                for s in range(4):
                    v.scalar_tensor_tensor(nh[s][:], inva[s][:], -1.0,
                                           bbc[s][:], ALU.mult, ALU.mult)
                for s in range(4):
                    v.tensor_tensor(t3q[s][:], nh[s][:], bgq[s][:], ALU.mult)
                for s in range(4):
                    v.tensor_tensor(
                        RT[s][:].rearrange("p (c two) -> p c two", two=2)[:, :, 0],
                        c1q[s][:], t3q[s][:], ALU.add)
                for s in range(4):
                    v.memset(RT[s][:, 1:2], 0.0)
                for s in range(4):
                    v.tensor_tensor(KQm[s][:],
                                    ps_mkq[0:64, 256 + s * 64:256 + (s + 1) * 64],
                                    triu_t[:], ALU.mult)
                chain.bump_d(v)
            block.vector(v_c2)

            def v_solve(v, ci=ci):
                chain.wait(v)
                for i in range(1, C2C):
                    for s in range(4):
                        in0 = Mbc[s][:, i * C2C:i * C2C + i].rearrange(
                            "p c -> p c ()").broadcast_to([128, i, 2])
                        v.scalar_tensor_tensor(
                            scr[s][:, 0:2 * i], in0, nh[s][:, i:i + 1],
                            RT[s][:, 0:2 * i], ALU.mult, ALU.mult,
                            accum_out=RT[s][:, 2 * i + 1:2 * i + 2])
                for s in range(4):
                    v.tensor_tensor(
                        Wbf[s][:],
                        RT[s][:].rearrange("p (c two) -> p c two", two=2)[:, :, 0],
                        RT[s][:].rearrange("p (c two) -> p c two", two=2)[:, :, 1],
                        ALU.add)
                chain.bump_d(v)
            block.vector(v_solve)

            def t_wt(t, ci=ci):
                chain.wait(t)
                for s in range(4):
                    t.transpose(ps_t1[0:64, 512 + s * 128:512 + (s + 1) * 128],
                                Wbf[s][:], ident_t[:])
                chain.bump_d(t)
            block.tensor(t_wt)

            def s_ev2(s_, ci=ci):
                chain.wait(s_)
                for s in range(4):
                    s_.copy(Wtime[s][:], ps_t1[0:64, 512 + s * 128:512 + (s + 1) * 128])
                chain.bump_d(s_)
            block.scalar(s_ev2)

            def t_mm2(t, ci=ci):
                chain.wait(t)
                for s in range(4):
                    t.matmul(ps_wkq[:, s * 64:(s + 1) * 64], Wtime[s][:],
                             KQm[s][:], start=True, stop=True)
                    t.matmul(ps_u[:, s * 128:(s + 1) * 128], Wtime[s][:],
                             Ktime[s][:], start=True, stop=True)
                chain.bump_d(t)
            block.tensor(t_mm2)

            def v_o(v, ci=ci, bi=bi):
                chain.wait(v)
                osl = slice(bi * C2C, (bi + 1) * C2C)
                for s in range(4):
                    v.tensor_tensor(t1q[s][:], Graw[s][:],
                                    ps_bp[:, s * 128 + 64:(s + 1) * 128],
                                    ALU.mult)
                for s in range(4):
                    v.tensor_tensor(t3q[s][:], Gp[s][:],
                                    ps_wkq[:, s * 64:(s + 1) * 64], ALU.mult)
                for s in range(4):
                    v.tensor_tensor(oblk[s][:, osl], t1q[s][:], t3q[s][:],
                                    ALU.add)
                for s in range(4):
                    v.tensor_scalar(t4q[s][:], ps_u[:, s * 128:(s + 1) * 128],
                                    Gp[s][:, C2C - 1:C2C], None, ALU.mult)
                for s in range(4):
                    v.scalar_tensor_tensor(Sst[s][:], Sst[s][:],
                                           Graw[s][:, C2C - 1:C2C], t4q[s][:],
                                           ALU.mult, ALU.add)
                chain.bump_d(v)
            block.vector(v_o)

            def t_strans(t, ci=ci):
                chain.wait(t)
                for s in range(4):
                    t.transpose(ps_t2[:, s * 128:(s + 1) * 128], Sst[s][:],
                                ident_t[:])
                chain.bump_d(t)
            block.tensor(t_strans)

            def s_ev3(s_, ci=ci):
                chain.wait(s_)
                for s in range(4):
                    s_.copy(SstT[s][:], ps_t2[:, s * 128:(s + 1) * 128])
                chain.bump_d(s_)
            block.scalar(s_ev3)

            if bi == BLK - 1:
                def g_oout(gp, ci=ci):
                    chain.wait(gp)
                    for s in range(4):
                        rs, cs = srcsl(s, ci - BLK + 1, BLK * C2C)
                        sto.start(gp, oraw[rs, cs], oblk[s][:])
                    chain.bump_d(gp)
                block.gpsimd(g_oout)

        for s in range(4):
            for blk in range(T // TG):
                j, b = s % HPC, s // HPC
                rs = slice(j * 128, (j + 1) * 128)
                cs = slice(b * T + blk * TG, b * T + blk * TG + TG)

                def g_pin(gp, s=s, rs=rs, cs=cs):
                    chain.wait(gp)
                    sto.wait_all(gp)
                    dma.start(gp, oblk[s][:], oraw[rs, cs])
                    dma.start(gp, gblk[:], ins["g"][rs, cs])
                block.gpsimd(g_pin)

                def s_osq(s_, s=s):
                    dma.wait_all(s_)
                    chain.wait(s_)
                    s_.activation(osq[:], oblk[s][:], AF.Square)
                    chain.bump_d(s_)
                block.scalar(s_osq)

                def t_ost(t, s=s):
                    chain.wait(t)
                    for k in range(1):
                        t.matmul(ps_post[0:1, :], ones_c[:], oblk[s][:],
                                 start=True, stop=True)
                        t.matmul(ps_post[32:33, :], ones_c[:], osq[:],
                                 start=True, stop=True)
                    chain.bump_d(t)
                block.tensor(t_ost)

                def s_orow(s_, s=s):
                    chain.wait(s_)
                    s_.activation(prow[:, 0:TG], ps_post[0:1, :], AF.Copy,
                                  scale=1.0 / 128)
                    s_.activation(prow[:, TG:2 * TG], ps_post[32:33, :], AF.Copy,
                                  scale=1.0 / 128)
                    chain.bump_d(s_)
                block.scalar(s_orow)

                def v_ovar(v, s=s):
                    chain.wait(v)
                    v.tensor_tensor(prow[:, 2 * TG:3 * TG], prow[:, 0:TG],
                                    prow[:, 0:TG], ALU.mult)
                    v.drain()
                    v.tensor_scalar(prow[:, 2 * TG:3 * TG], prow[:, 2 * TG:3 * TG],
                                    1.0, -LN_EPS, ALU.mult, ALU.add)
                    v.drain()
                    v.tensor_tensor(prow[:, 3 * TG:4 * TG], prow[:, TG:2 * TG],
                                    prow[:, 2 * TG:3 * TG], ALU.subtract)
                    chain.bump_d(v)
                block.vector(v_ovar)

                def s_orstd(s_, s=s):
                    chain.wait(s_)
                    s_.activation(prow[:, 4 * TG:5 * TG], prow[:, 3 * TG:4 * TG],
                                  AF.Ln)
                    s_.drain()
                    s_.activation(prow[:, 5 * TG:6 * TG], prow[:, 4 * TG:5 * TG],
                                  AF.Exp, scale=-0.5)
                    chain.bump_d(s_)
                block.scalar(s_orstd)

                def v_omrs(v, s=s):
                    chain.wait(v)
                    v.tensor_tensor(prow[:, 6 * TG:7 * TG], prow[:, 0:TG],
                                    prow[:, 5 * TG:6 * TG], ALU.mult)
                    chain.bump_d(v)
                block.vector(v_omrs)

                def g_obc(gp, s=s):
                    chain.wait(gp)
                    m0 = DmaCtr(ld)
                    dma.start(gp, orow[s:s + 1, :], prow[:, 5 * TG:7 * TG])
                    gp.wait_ge(ld, dma.n)
                    dma.start(gp, rstdbc2[:],
                              orow[s:s + 1, 0:TG].broadcast_to([128, TG]))
                    dma.start(gp, mrsbc2[:],
                              orow[s:s + 1, TG:2 * TG].broadcast_to([128, TG]))
                block.gpsimd(g_obc)

                def v_ofin(v, s=s):
                    dma.wait_all(v)
                    chain.wait(v)
                    v.tensor_tensor(ot1[:], oblk[s][:], rstdbc2[:], ALU.mult)
                    v.drain()
                    v.tensor_tensor(ot1[:], ot1[:], mrsbc2[:], ALU.subtract)
                    v.drain()
                    v.tensor_scalar(ot1[:], ot1[:], lnw_t[:], lnb_t[:],
                                    ALU.mult, ALU.add)
                    v.drain()
                    v.tensor_tensor(oout[:], ot1[:], gblk[:], ALU.mult)
                    chain.bump_d(v)
                block.vector(v_ofin)

                def g_pout(gp, s=s, rs=rs, cs=cs):
                    chain.wait(gp)
                    sto.start(gp, oTo[rs, cs], oout[:])
                    chain.bump_d(gp)
                block.gpsimd(g_pout)

        def fin(sync):
            chain.wait(sync)
            sto.wait_all(sync)
        block.sync(fin)

    return nc


FE = 16
FF = 768
FFT = FF // 128


def build_launch3(dbg=False):
    nc = bass.Bass("TRN2", target_bir_lowering=False)
    xnT = nc.dram_tensor("xnT", [D, TSL], BF16, kind="ExternalInput").ap()
    rw = nc.dram_tensor("rw", [D, FE], F32, kind="ExternalInput").ap()
    gw = nc.dram_tensor("gw", [FE, D, FF], F32, kind="ExternalInput").ap()
    uw = nc.dram_tensor("uw", [FE, D, FF], F32, kind="ExternalInput").ap()
    dwt = nc.dram_tensor("dwt", [4, FE, FFT, 128, 512], F32,
                         kind="ExternalInput").ap()
    ident = nc.dram_tensor("ident", [128, 128], BF16, kind="ExternalInput").ap()
    kind_d = "ExternalOutput" if dbg else "Internal"
    hdram = nc.dram_tensor("hdram", [FE, FF, TSL], BF16, kind=kind_d).ap()
    combdram = nc.dram_tensor("combdram", [FE, TSL], BF16, kind=kind_d).ap()
    ffnTo = nc.dram_tensor("ffnTo", [D, TSL], F32, kind="ExternalOutput").ap()

    with ExitStack() as ctx:
        E = ctx.enter_context
        xn = E(nc.sbuf_tensor("xn", [128, KC * TSL], BF16))
        rw_bf = E(nc.sbuf_tensor("rw_bf", [128, KC * FE], BF16))
        ident_t = E(nc.sbuf_tensor("ident_t", [128, 128], BF16))
        scor_sb = E(nc.sbuf_tensor("scor_sb", [FE, TSL], BF16))
        scTok = E(nc.sbuf_tensor("scTok", [128, 8 * FE], F32))
        smx = E(nc.sbuf_tensor("smx", [128, 8 * FE], F32))
        cols = E(nc.sbuf_tensor("cols", [128, 24], F32))
        comb = E(nc.sbuf_tensor("comb", [128, 8 * FE], BF16))
        combT_sb = E(nc.sbuf_tensor("combT_sb", [FE, TSL], BF16))
        gwb = [E(nc.sbuf_tensor(f"gwb{i}", [128, KC * FF], BF16)) for i in range(2)]
        uwb = [E(nc.sbuf_tensor(f"uwb{i}", [128, KC * FF], BF16)) for i in range(2)]
        silu_sb = E(nc.sbuf_tensor("silu_sb", [128, TG], BF16))
        h1 = E(nc.sbuf_tensor("h1", [128, TG], F32))
        hbf = E(nc.sbuf_tensor("hbf", [128, TG], BF16))
        cbc = E(nc.sbuf_tensor("cbc", [128, TSL], BF16))
        dwb = [E(nc.sbuf_tensor(f"dwb{i}", [128, FFT * 512], BF16))
               for i in range(2)]
        hin = E(nc.sbuf_tensor("hin", [128, TSL], BF16))
        ffout = E(nc.sbuf_tensor("ffout", [128, 8 * TG], F32))

        ps = [E(nc.psum_tensor(f"psf{i}", [128, TG], F32)) for i in range(7)]
        pst = E(nc.psum_tensor("pstf", [128, 1024], BF16))

        ld = E(nc.semaphore("ld"))
        wfree = E(nc.semaphore("wfree"))
        ch_sem = E(nc.semaphore("ch"))
        st = E(nc.semaphore("st"))
        block = E(nc.Block())

        dma = DmaCtr(ld)
        sto = DmaCtr(st)
        chain = Chain(ch_sem)
        AX = mybir.AxisListType.X
        wfree_n = [0]

        def g_loads(gp):
            dma.start(gp, xn[:].rearrange("p (k t) -> p k t", k=KC),
                      xnT.rearrange("(k p) t -> p k t", p=128))
            dma.start(gp, rw_bf[:].rearrange("p (k c) -> p k c", k=KC),
                      rw.rearrange("(k p) c -> p k c", p=128))
            dma.start(gp, ident_t[:], ident)
        block.gpsimd(g_loads)

        def t_r(t):
            dma.wait_all(t)
            for tt in range(8):
                for k in range(KC):
                    t.matmul(ps[5][0:128, tt * 16:(tt + 1) * 16],
                             xn[:, k * TSL + tt * 128:k * TSL + (tt + 1) * 128],
                             rw_bf[:, k * FE:(k + 1) * FE],
                             start=(k == 0), stop=(k == KC - 1))
            chain.bump_d(t)
        block.tensor(t_r)

        def s_r(s):
            chain.wait(s)
            s.copy(scTok[:], ps[5][0:128, 0:128])
            chain.bump_d(s)
        block.scalar(s_r)

        def v_sm(v):
            chain.wait(v)
            def tl(w):
                return scTok[:, w * FE:(w + 1) * FE]
            def sm(w):
                return smx[:, w * FE:(w + 1) * FE]
            for w in range(8):
                v.tensor_reduce(cols[:, w:w + 1], tl(w), AX, ALU.max)
            for w in range(8):
                v.tensor_scalar(sm(w), tl(w), cols[:, w:w + 1], None,
                                ALU.subtract)
            chain.bump_d(v)
        block.vector(v_sm)

        def s_sm(s):
            chain.wait(s)
            s.activation(smx[:], smx[:], AF.Exp)
            chain.bump_d(s)
        block.scalar(s_sm)

        def v_top2(v):
            chain.wait(v)
            def sm(w):
                return smx[:, w * FE:(w + 1) * FE]
            def mk(w):
                return scTok[:, w * FE:(w + 1) * FE]
            def cm(w):
                return comb[:, w * FE:(w + 1) * FE]
            for w in range(8):
                v.tensor_reduce(cols[:, 8 + w:9 + w], sm(w), AX, ALU.max)
            for w in range(8):
                v.tensor_scalar(mk(w), sm(w), cols[:, 8 + w:9 + w], None,
                                ALU.is_equal)
            for w in range(8):
                v.tensor_scalar(cm(w), mk(w), cols[:, 8 + w:9 + w], None,
                                ALU.mult)
            v.drain()
            for w in range(8):
                v.tensor_tensor(sm(w), sm(w), cm(w), ALU.subtract)
            v.drain()
            for w in range(8):
                v.tensor_reduce(cols[:, 16 + w:17 + w], sm(w), AX, ALU.max)
            v.drain()
            for w in range(8):
                v.tensor_tensor(cols[:, w:w + 1], cols[:, 8 + w:9 + w],
                                cols[:, 16 + w:17 + w], ALU.add)
            v.drain()
            for w in range(8):
                v.reciprocal(cols[:, w:w + 1], cols[:, w:w + 1])
            v.drain()
            for w in range(8):
                v.tensor_scalar(sm(w), sm(w), cols[:, 16 + w:17 + w], None,
                                ALU.is_equal)
            for w in range(8):
                v.tensor_tensor(cols[:, 8 + w:9 + w], cols[:, 8 + w:9 + w],
                                cols[:, w:w + 1], ALU.mult)
            for w in range(8):
                v.tensor_tensor(cols[:, 16 + w:17 + w], cols[:, 16 + w:17 + w],
                                cols[:, w:w + 1], ALU.mult)
            v.drain()
            for w in range(8):
                v.tensor_scalar(cm(w), mk(w), cols[:, 8 + w:9 + w], None,
                                ALU.mult)
            for w in range(8):
                v.scalar_tensor_tensor(cm(w), sm(w), cols[:, 16 + w:17 + w],
                                       cm(w), ALU.mult, ALU.add)
            chain.bump_d(v)
        block.vector(v_top2)

        def t_ct(t):
            chain.wait(t)
            for wi in range(8):
                t.transpose(pst[0:16, wi * 128:(wi + 1) * 128],
                            comb[:, wi * FE:(wi + 1) * FE], ident_t[:])
            chain.bump_d(t)
        block.tensor(t_ct)

        def s_ct(s):
            chain.wait(s)
            for wi in range(8):
                s.copy(combT_sb[:, wi * 128:(wi + 1) * 128],
                       pst[0:16, wi * 128:(wi + 1) * 128])
            chain.bump_d(s)
        block.scalar(s_ct)

        def g_ct(gp):
            chain.wait(gp)
            dma.start(gp, combdram, combT_sb[:])
            gp.wait_ge(ld, dma.n)
        block.gpsimd(g_ct)

        def g_w01(gp):
            for e in range(2):
                dma.start(gp, gwb[e][:].rearrange("p (k c) -> p k c", k=KC),
                          gw[e].rearrange("(k p) c -> p k c", p=128))
                dma.start(gp, uwb[e][:].rearrange("p (k c) -> p k c", k=KC),
                          uw[e].rearrange("(k p) c -> p k c", p=128))
        block.gpsimd(g_w01)

        for e in range(FE):
            def g_cbc(gp, e=e):
                chain.wait(gp)
                dma.start(gp, cbc[:], combdram[e:e + 1, :].broadcast_to([128, TSL]))
            block.gpsimd(g_cbc)

            for ft in range(FFT):
                def t_gu(t, e=e, ft=ft):
                    dma.wait_all(t)
                    chain.wait(t)
                    for h in range(2):
                        for k in range(KC):
                            t.matmul(ps[0 + h][:, :],
                                     gwb[e % 2][:, k * FF + ft * 128:k * FF + (ft + 1) * 128],
                                     xn[:, k * TSL + h * TG:k * TSL + (h + 1) * TG],
                                     start=(k == 0), stop=(k == KC - 1))
                        for k in range(KC):
                            t.matmul(ps[2 + h][:, :],
                                     uwb[e % 2][:, k * FF + ft * 128:k * FF + (ft + 1) * 128],
                                     xn[:, k * TSL + h * TG:k * TSL + (h + 1) * TG],
                                     start=(k == 0), stop=(k == KC - 1))
                    chain.bump_d(t)
                block.tensor(t_gu)

                for h in range(2):
                    def s_h(s, e=e, ft=ft, h=h):
                        chain.wait(s)
                        s.activation(silu_sb[:], ps[0 + h][:, :], AF.Silu)
                        chain.bump_d(s)
                    block.scalar(s_h)

                    def v_h(v, e=e, ft=ft, h=h):
                        chain.wait(v)
                        v.tensor_tensor(h1[:], ps[2 + h][:, :], silu_sb[:],
                                        ALU.mult)
                        v.drain()
                        v.tensor_tensor(hbf[:], h1[:],
                                        cbc[:, h * TG:(h + 1) * TG], ALU.mult)
                        chain.bump_d(v)
                    block.vector(v_h)

                    def g_h(gp, e=e, ft=ft, h=h):
                        chain.wait(gp)
                        sto.start(gp, hdram[e, ft * 128:(ft + 1) * 128,
                                            h * TG:(h + 1) * TG], hbf[:])
                        chain.bump_d(gp)
                    block.gpsimd(g_h)

            def t_wfree(t, e=e):
                chain.wait(t)
                t.nop().then_inc(wfree, 1)
                wfree_n[0] += 1
            block.tensor(t_wfree)

            if e + 2 < FE:
                def g_wnext(gp, e=e, tgt=wfree_n[0]):
                    gp.wait_ge(wfree, tgt)
                    dma.start(gp, gwb[e % 2][:].rearrange("p (k c) -> p k c", k=KC),
                              gw[e + 2].rearrange("(k p) c -> p k c", p=128))
                    dma.start(gp, uwb[e % 2][:].rearrange("p (k c) -> p k c", k=KC),
                              uw[e + 2].rearrange("(k p) c -> p k c", p=128))
                block.gpsimd(g_wnext)

        for quad in range(4):
            for h in range(2):
                def g_dw01(gp, quad=quad, h=h):
                    chain.wait(gp)
                    sto.wait_all(gp)
                    for e in range(2):
                        dma.start(gp, dwb[e][:].rearrange("p (f c) -> p f c", f=FFT),
                                  dwt[quad, e].rearrange("f p c -> p f c"))
                block.gpsimd(g_dw01)

                for e in range(FE):
                    for ft in range(FFT):
                        def g_h2(gp, quad=quad, h=h, e=e, ft=ft):
                            chain.wait(gp)
                            dma.start(gp, hin[:, 0:TG],
                                      hdram[e, ft * 128:(ft + 1) * 128,
                                            h * TG:(h + 1) * TG])
                        block.gpsimd(g_h2)

                        def t_dn(t, quad=quad, h=h, e=e, ft=ft):
                            dma.wait_all(t)
                            chain.wait(t)
                            first = (e == 0 and ft == 0)
                            last = (e == FE - 1 and ft == FFT - 1)
                            for dt in range(4):
                                t.matmul(ps[dt][:, :],
                                         dwb[e % 2][:, ft * 512 + dt * 128:
                                                     ft * 512 + (dt + 1) * 128],
                                         hin[:, 0:TG],
                                         start=first, stop=last)
                            chain.bump_d(t)
                        block.tensor(t_dn)

                    def t_dfree(t, quad=quad, h=h, e=e):
                        chain.wait(t)
                        t.nop().then_inc(wfree, 1)
                        wfree_n[0] += 1
                    block.tensor(t_dfree)

                    if e + 2 < FE:
                        def g_dwn(gp, quad=quad, h=h, e=e, tgt=wfree_n[0]):
                            gp.wait_ge(wfree, tgt)
                            dma.start(gp, dwb[e % 2][:].rearrange(
                                "p (f c) -> p f c", f=FFT),
                                dwt[quad, e + 2].rearrange("f p c -> p f c"))
                        block.gpsimd(g_dwn)

                def s_dev(s, quad=quad, h=h):
                    chain.wait(s)
                    for dt in range(4):
                        s.copy(ffout[:, (dt * 2 + h) * TG:(dt * 2 + h + 1) * TG],
                               ps[dt][:, :])
                    chain.bump_d(s)
                block.scalar(s_dev)

                def g_fout(gp, quad=quad, h=h):
                    chain.wait(gp)
                    for dt in range(4):
                        sto.start(gp, ffnTo[(quad * 4 + dt) * 128:
                                            (quad * 4 + dt + 1) * 128,
                                            h * TG:(h + 1) * TG],
                                  ffout[:, (dt * 2 + h) * TG:(dt * 2 + h + 1) * TG])
                    chain.bump_d(gp)
                block.gpsimd(g_fout)

        def fin(sync):
            chain.wait(sync)
            sto.wait_all(sync)
        block.sync(fin)

    return nc



def build_launch4():
    nc = bass.Bass("TRN2", target_bir_lowering=False)
    oT = nc.dram_tensor("oT", [D, TSL], BF16, kind="ExternalInput").ap()
    woT = nc.dram_tensor("woT", [D, D], F32, kind="ExternalInput").ap()
    xT = nc.dram_tensor("xTs", [D, TSL], F32, kind="ExternalInput").ap()
    ffnT = nc.dram_tensor("ffnT", [D, TSL], F32, kind="ExternalInput").ap()
    onescol_bf = nc.dram_tensor("onescol_bf", [128, 1], BF16,
                                kind="ExternalInput").ap()
    rowdram = nc.dram_tensor("rowdram", [2, TSL], F32, kind="Internal").ap()
    x1To = nc.dram_tensor("x1To", [D, TSL], F32, kind="ExternalOutput").ap()
    xn1To = nc.dram_tensor("xn1To", [D, TSL], BF16, kind="ExternalOutput").ap()

    with ExitStack() as ctx:
        E = ctx.enter_context
        ot = E(nc.sbuf_tensor("ot", [128, KC * TSL], BF16))
        wo_bf = E(nc.sbuf_tensor("wo_bf", [128, KC * D], BF16))
        ones_c = E(nc.sbuf_tensor("ones_c", [128, 1], BF16))
        xch = E(nc.sbuf_tensor("xch", [128, TSL], F32))
        fch = E(nc.sbuf_tensor("fch", [128, TSL], F32))
        x1ch = E(nc.sbuf_tensor("x1ch", [128, TSL], F32))
        x1bf = E(nc.sbuf_tensor("x1bf", [128, TSL], BF16))
        x1sq = E(nc.sbuf_tensor("x1sq", [128, TSL], BF16))
        att = E(nc.sbuf_tensor("att", [128, TSL], F32))
        rowsb = E(nc.sbuf_tensor("rowsb", [1, 8 * TSL], F32))
        rstdbc = E(nc.sbuf_tensor("rstdbc", [128, TSL], F32))
        mrsbc = E(nc.sbuf_tensor("mrsbc", [128, TSL], F32))
        xn1ch = E(nc.sbuf_tensor("xn1ch", [128, TSL], BF16))

        ps = [E(nc.psum_tensor(f"ps4_{i}", [128, TG], F32)) for i in range(8)]

        ld = E(nc.semaphore("ld"))
        ch_sem = E(nc.semaphore("ch"))
        st = E(nc.semaphore("st"))
        block = E(nc.Block())
        dma = DmaCtr(ld)
        sto = DmaCtr(st)
        chain = Chain(ch_sem)

        def g_loads(gp):
            dma.start(gp, ot[:].rearrange("p (k t) -> p k t", k=KC),
                      oT.rearrange("(k p) t -> p k t", p=128))
            dma.start(gp, wo_bf[:].rearrange("p (k c) -> p k c", k=KC),
                      woT.rearrange("(k p) c -> p k c", p=128))
        block.gpsimd(g_loads)

        def s_loads(sync):
            dma.start(sync, ones_c[:], onescol_bf)
        block.sync(s_loads)

        for dt in range(KC):
            def g_in(gp, dt=dt):
                chain.wait(gp)
                dma.start(gp, xch[:], xT[dt * 128:(dt + 1) * 128, :])
                dma.start(gp, fch[:], ffnT[dt * 128:(dt + 1) * 128, :])
            block.gpsimd(g_in)

            def t_wo(t, dt=dt):
                dma.wait_all(t)
                chain.wait(t)
                for h in range(2):
                    for k in range(KC):
                        t.matmul(ps[h][:, :],
                                 wo_bf[:, k * D + dt * 128:k * D + (dt + 1) * 128],
                                 ot[:, k * TSL + h * TG:k * TSL + (h + 1) * TG],
                                 start=(k == 0), stop=(k == KC - 1))
                chain.bump_d(t)
            block.tensor(t_wo)

            def s_att(s, dt=dt):
                chain.wait(s)
                for h in range(2):
                    s.copy(att[:, h * TG:(h + 1) * TG], ps[h][:, :])
                chain.bump_d(s)
            block.scalar(s_att)

            def v_res(v, dt=dt):
                chain.wait(v)
                v.tensor_tensor(x1ch[:], xch[:], att[:], ALU.add)
                v.drain()
                v.tensor_tensor(x1ch[:], x1ch[:], fch[:], ALU.add)
                v.drain()
                v.tensor_copy(x1bf[:], x1ch[:])
                chain.bump_d(v)
            block.vector(v_res)

            def s_sq(s, dt=dt):
                chain.wait(s)
                s.activation(x1sq[:], x1bf[:], AF.Square)
                chain.bump_d(s)
            block.scalar(s_sq)

            def t_st(t, dt=dt):
                chain.wait(t)
                for h in range(2):
                    t.matmul(ps[4 + h][0:1, :], ones_c[:],
                             x1bf[:, h * TG:(h + 1) * TG],
                             start=(dt == 0), stop=(dt == KC - 1))
                    t.matmul(ps[4 + h][32:33, :], ones_c[:],
                             x1sq[:, h * TG:(h + 1) * TG],
                             start=(dt == 0), stop=(dt == KC - 1))
                chain.bump_d(t)
            block.tensor(t_st)

            def g_x1(gp, dt=dt):
                chain.wait(gp)
                sto.start(gp, x1To[dt * 128:(dt + 1) * 128, :], x1ch[:])
            block.gpsimd(g_x1)

        def s_rows(s):
            chain.wait(s)
            for h in range(2):
                s.activation(rowsb[:, h * TG:(h + 1) * TG], ps[4 + h][0:1, :],
                             AF.Copy, scale=1.0 / D)
                s.activation(rowsb[:, TSL + h * TG:TSL + (h + 1) * TG],
                             ps[4 + h][32:33, :], AF.Copy, scale=1.0 / D)
            chain.bump_d(s)
        block.scalar(s_rows)

        def v_var(v):
            chain.wait(v)
            v.tensor_tensor(rowsb[:, 2 * TSL:3 * TSL], rowsb[:, 0:TSL],
                            rowsb[:, 0:TSL], ALU.mult)
            v.drain()
            v.tensor_scalar(rowsb[:, 2 * TSL:3 * TSL], rowsb[:, 2 * TSL:3 * TSL],
                            1.0, -LN_EPS, ALU.mult, ALU.add)
            v.drain()
            v.tensor_tensor(rowsb[:, 3 * TSL:4 * TSL], rowsb[:, TSL:2 * TSL],
                            rowsb[:, 2 * TSL:3 * TSL], ALU.subtract)
            chain.bump_d(v)
        block.vector(v_var)

        def s_rstd(s):
            chain.wait(s)
            s.activation(rowsb[:, 4 * TSL:5 * TSL], rowsb[:, 3 * TSL:4 * TSL],
                         AF.Ln)
            s.drain()
            s.activation(rowsb[:, 5 * TSL:6 * TSL], rowsb[:, 4 * TSL:5 * TSL],
                         AF.Exp, scale=-0.5)
            chain.bump_d(s)
        block.scalar(s_rstd)

        def v_mrs(v):
            chain.wait(v)
            v.tensor_tensor(rowsb[:, 6 * TSL:7 * TSL], rowsb[:, 0:TSL],
                            rowsb[:, 5 * TSL:6 * TSL], ALU.mult)
            chain.bump_d(v)
        block.vector(v_mrs)

        def g_bc(gp):
            chain.wait(gp)
            dma.start(gp, rowdram, rowsb[:, 5 * TSL:7 * TSL])
            gp.wait_ge(ld, dma.n)
            dma.start(gp, rstdbc[:], rowdram[0:1, :].broadcast_to([128, TSL]))
            dma.start(gp, mrsbc[:], rowdram[1:2, :].broadcast_to([128, TSL]))
        block.gpsimd(g_bc)

        for dt in range(KC):
            def g_in2(gp, dt=dt):
                chain.wait(gp)
                sto.wait_all(gp)
                dma.start(gp, xch[:], x1To[dt * 128:(dt + 1) * 128, :])
            block.gpsimd(g_in2)

            def v_xn1(v, dt=dt):
                dma.wait_all(v)
                chain.wait(v)
                v.tensor_tensor(x1ch[:], xch[:], rstdbc[:], ALU.mult)
                v.drain()
                v.tensor_tensor(xn1ch[:], x1ch[:], mrsbc[:], ALU.subtract)
                chain.bump_d(v)
            block.vector(v_xn1)

            def g_o2(gp, dt=dt):
                chain.wait(gp)
                sto.start(gp, xn1To[dt * 128:(dt + 1) * 128, :], xn1ch[:])
            block.gpsimd(g_o2)

        def fin(sync):
            chain.wait(sync)
            sto.wait_all(sync)
        block.sync(fin)

    return nc


QW = 512
NQW = T // QW
SM_SHIFT = 12.0


def build_launch5():
    nc = bass.Bass("TRN2", target_bir_lowering=False)
    xn1 = nc.dram_tensor("xn1", [D, NTOK], BF16, kind="ExternalInput").ap()
    wq = nc.dram_tensor("wq", [D, CH], F32, kind="ExternalInput").ap()
    wk = nc.dram_tensor("wk", [D, CH], F32, kind="ExternalInput").ap()
    wv = nc.dram_tensor("wv", [D, CH], F32, kind="ExternalInput").ap()
    ident = nc.dram_tensor("ident", [128, 128], BF16, kind="ExternalInput").ap()
    onescol_bf = nc.dram_tensor("onescol_bf", [128, 1], BF16,
                                kind="ExternalInput").ap()
    cmask = nc.dram_tensor("cmask", [4 * 128, QW], BF16, kind="ExternalInput").ap()
    rowdram5 = nc.dram_tensor("rowdram5", [1, QW], F32, kind="Internal").ap()
    o1To = nc.dram_tensor("o1To", [CH, NTOK], BF16, kind="ExternalOutput").ap()

    with ExitStack() as ctx:
        E = ctx.enter_context
        wqb = E(nc.sbuf_tensor("wqb", [128, KC * CH], BF16))
        wkb = E(nc.sbuf_tensor("wkb", [128, KC * CH], BF16))
        wvb = E(nc.sbuf_tensor("wvb", [128, KC * CH], BF16))
        ident_t = E(nc.sbuf_tensor("ident_t", [128, 128], BF16))
        ones_c = E(nc.sbuf_tensor("ones_c", [128, 1], BF16))
        cmask_t = E(nc.sbuf_tensor("cmask_t", [128, 4 * QW], BF16))
        xw = E(nc.sbuf_tensor("xw", [128, KC * QW], BF16))
        qT = [E(nc.sbuf_tensor(f"qT{s}", [128, T], BF16)) for s in range(4)]
        kT = [E(nc.sbuf_tensor(f"kT{s}", [128, T], BF16)) for s in range(4)]
        vt = [E(nc.sbuf_tensor(f"vt{s}", [128, T], BF16)) for s in range(4)]
        vev = E(nc.sbuf_tensor("vev", [128, QW], BF16))
        pexp = E(nc.sbuf_tensor("pexp", [128, QW], BF16))
        neg12 = E(nc.sbuf_tensor("neg12", [128, 1], F32))
        srow = E(nc.sbuf_tensor("srow", [1, QW], F32))
        rbc = E(nc.sbuf_tensor("rbc", [128, QW], F32))
        oout = E(nc.sbuf_tensor("oout5", [128, QW], BF16))

        ps_q = E(nc.psum_tensor("ps_q", [128, QW], F32))
        ps_k = E(nc.psum_tensor("ps_k", [128, QW], F32))
        ps_v = E(nc.psum_tensor("ps_v", [128, QW], F32))
        ps_sc = E(nc.psum_tensor("ps_sc", [128, QW], F32))
        ps_o = E(nc.psum_tensor("ps_o", [128, QW], F32))
        ps_sum = E(nc.psum_tensor("ps_sum", [128, QW], F32))
        ps_tr = E(nc.psum_tensor("ps_tr", [128, 512], BF16))

        ld = E(nc.semaphore("ld"))
        ch_sem = E(nc.semaphore("ch"))
        st = E(nc.semaphore("st"))
        block = E(nc.Block())
        dma = DmaCtr(ld)
        sto = DmaCtr(st)
        chain = Chain(ch_sem)

        def g_loads(gp):
            for wt, wsrc in ((wqb, wq), (wkb, wk), (wvb, wv)):
                dma.start(gp, wt[:].rearrange("p (k c) -> p k c", k=KC),
                          wsrc.rearrange("(k p) c -> p k c", p=128))
        block.gpsimd(g_loads)

        def s_loads(sync):
            dma.start(sync, ident_t[:], ident)
            dma.start(sync, ones_c[:], onescol_bf)
            dma.start(sync, cmask_t[:].rearrange("p (m w) -> p m w", m=4),
                      cmask.rearrange("(m p) w -> p m w", p=128))
        block.sync(s_loads)

        def v_init(v):
            dma.wait_all(v)
            v.memset(neg12[:], -SM_SHIFT)
            chain.bump_d(v)
        block.vector(v_init)

        for w in range(NTOK // QW):
            def g_xw(gp, w=w):
                chain.wait(gp)
                dma.start(gp, xw[:].rearrange("p (k t) -> p k t", k=KC),
                          xn1.rearrange("(k p) t -> p k t", p=128)
                          [:, :, w * QW:(w + 1) * QW])
            block.gpsimd(g_xw)

            for j in range(HPC):
                s = j + HPC * (w // NQW)
                tw = (w % NQW) * QW

                def t_qkv(t, w=w, j=j):
                    dma.wait_all(t)
                    chain.wait(t)
                    for k in range(KC):
                        t.matmul(ps_q[:, :],
                                 wqb[:, k * CH + j * 128:k * CH + (j + 1) * 128],
                                 xw[:, k * QW:(k + 1) * QW],
                                 start=(k == 0), stop=(k == KC - 1))
                    for k in range(KC):
                        t.matmul(ps_k[:, :],
                                 wkb[:, k * CH + j * 128:k * CH + (j + 1) * 128],
                                 xw[:, k * QW:(k + 1) * QW],
                                 start=(k == 0), stop=(k == KC - 1))
                    for k in range(KC):
                        t.matmul(ps_v[:, :],
                                 wvb[:, k * CH + j * 128:k * CH + (j + 1) * 128],
                                 xw[:, k * QW:(k + 1) * QW],
                                 start=(k == 0), stop=(k == KC - 1))
                    chain.bump_d(t)
                block.tensor(t_qkv)

                def s_ev(s_, s=s, tw=tw):
                    chain.wait(s_)
                    s_.copy(qT[s][:, tw:tw + QW], ps_q[:, :])
                    s_.copy(kT[s][:, tw:tw + QW], ps_k[:, :])
                    s_.copy(vev[:], ps_v[:, :])
                    chain.bump_d(s_)
                block.scalar(s_ev)

                def t_vt(t, s=s, tw=tw):
                    chain.wait(t)
                    for c4 in range(4):
                        t.transpose(ps_tr[0:128, c4 * 128:(c4 + 1) * 128],
                                    vev[:, c4 * 128:(c4 + 1) * 128], ident_t[:])
                    chain.bump_d(t)
                block.tensor(t_vt)

                def s_vt(s_, s=s, tw=tw):
                    chain.wait(s_)
                    s_.copy(vt[s][:, tw:tw + QW], ps_tr[0:128, 0:QW])
                    chain.bump_d(s_)
                block.scalar(s_vt)

        for s in range(4):
            for qw in range(NQW):
                nkv = (qw + 1) * 4
                for kv in range(nkv):
                    def t_sc(t, s=s, qw=qw, kv=kv):
                        chain.wait(t)
                        t.matmul(ps_sc[:, :], kT[s][:, kv * 128:(kv + 1) * 128],
                                 qT[s][:, qw * QW:(qw + 1) * QW],
                                 start=True, stop=True)
                        chain.bump_d(t)
                    block.tensor(t_sc)

                    def s_exp(s_, s=s, qw=qw, kv=kv):
                        chain.wait(s_)
                        s_.activation(pexp[:], ps_sc[:, :], AF.Exp,
                                      bias=neg12[:], scale=float(DH) ** -0.5)
                        chain.bump_d(s_)
                    block.scalar(s_exp)

                    if kv >= qw * 4:
                        m = kv - qw * 4
                        def v_mask(v, m=m):
                            chain.wait(v)
                            v.tensor_tensor(pexp[:], pexp[:],
                                            cmask_t[:, m * QW:(m + 1) * QW],
                                            ALU.mult)
                            chain.bump_d(v)
                        block.vector(v_mask)

                    def t_pv(t, s=s, qw=qw, kv=kv, nkv=nkv):
                        chain.wait(t)
                        t.matmul(ps_o[:, :], vt[s][:, kv * 128:(kv + 1) * 128],
                                 pexp[:], start=(kv == 0), stop=(kv == nkv - 1))
                        t.matmul(ps_sum[0:1, :], ones_c[:], pexp[:],
                                 start=(kv == 0), stop=(kv == nkv - 1))
                        chain.bump_d(t)
                    block.tensor(t_pv)

                def s_sumrow(s_, s=s, qw=qw):
                    chain.wait(s_)
                    s_.activation(srow[:], ps_sum[0:1, :], AF.Ln)
                    s_.drain()
                    s_.activation(srow[:], srow[:], AF.Exp, scale=-1.0)
                    chain.bump_d(s_)
                block.scalar(s_sumrow)

                def g_rbc(gp, s=s, qw=qw):
                    chain.wait(gp)
                    dma.start(gp, rowdram5, srow[:])
                    gp.wait_ge(ld, dma.n)
                    dma.start(gp, rbc[:], rowdram5[0:1, :].broadcast_to([128, QW]))
                block.gpsimd(g_rbc)

                def v_fin(v, s=s, qw=qw):
                    dma.wait_all(v)
                    chain.wait(v)
                    v.tensor_tensor(oout[:], ps_o[:, :], rbc[:], ALU.mult)
                    chain.bump_d(v)
                block.vector(v_fin)

                def g_out(gp, s=s, qw=qw):
                    chain.wait(gp)
                    j, b = s % HPC, s // HPC
                    sto.start(gp, o1To[j * 128:(j + 1) * 128,
                                       b * T + qw * QW:b * T + (qw + 1) * QW],
                              oout[:])
                    chain.bump_d(gp)
                block.gpsimd(g_out)

        def fin(sync):
            chain.wait(sync)
            sto.wait_all(sync)
        block.sync(fin)

    return nc



def build_launch6():
    nc = bass.Bass("TRN2", target_bir_lowering=False)
    o1 = nc.dram_tensor("o1", [D, TSL], BF16, kind="ExternalInput").ap()
    wo = nc.dram_tensor("wo6", [D, D], F32, kind="ExternalInput").ap()
    w0T = nc.dram_tensor("w0T", [D, D], F32, kind="ExternalInput").ap()
    eb0 = nc.dram_tensor("eb0", [128, KC], F32, kind="ExternalInput").ap()
    xn1T = nc.dram_tensor("xn1Ts", [D, TSL], BF16, kind="ExternalInput").ap()
    x1T = nc.dram_tensor("x1Ts", [D, TSL], F32, kind="ExternalInput").ap()
    outTo = nc.dram_tensor("outTo", [D, TSL], F32, kind="ExternalOutput").ap()

    with ExitStack() as ctx:
        E = ctx.enter_context
        o1t = E(nc.sbuf_tensor("o1t", [128, KC * TSL], BF16))
        xn1t = E(nc.sbuf_tensor("xn1t", [128, KC * TSL], BF16))
        eb0_t = E(nc.sbuf_tensor("eb0_t", [128, KC], F32))
        wob = [E(nc.sbuf_tensor(f"wob{i}", [128, KC * 128], BF16)) for i in range(2)]
        w0b = [E(nc.sbuf_tensor(f"w0b{i}", [128, KC * 128], BF16)) for i in range(2)]
        xch = E(nc.sbuf_tensor("xch6", [128, TSL], F32))
        attc = E(nc.sbuf_tensor("attc", [128, TSL], F32))
        ffnc = E(nc.sbuf_tensor("ffnc", [128, TSL], F32))
        outc = E(nc.sbuf_tensor("outc", [128, TSL], F32))

        ps = [E(nc.psum_tensor(f"ps6_{i}", [128, TG], F32)) for i in range(8)]

        ld = E(nc.semaphore("ld"))
        wfree = E(nc.semaphore("wfree6"))
        ch_sem = E(nc.semaphore("ch"))
        st = E(nc.semaphore("st"))
        block = E(nc.Block())
        dma = DmaCtr(ld)
        sto = DmaCtr(st)
        chain = Chain(ch_sem)
        wfree_n = [0]

        def g_loads(gp):
            dma.start(gp, o1t[:].rearrange("p (k t) -> p k t", k=KC),
                      o1.rearrange("(k p) t -> p k t", p=128))
            dma.start(gp, xn1t[:].rearrange("p (k t) -> p k t", k=KC),
                      xn1T.rearrange("(k p) t -> p k t", p=128))
            for dt in range(2):
                dma.start(gp, wob[dt][:].rearrange("p (k c) -> p k c", k=KC),
                          wo.rearrange("(k p) c -> p k c", p=128)
                          [:, :, dt * 128:(dt + 1) * 128])
                dma.start(gp, w0b[dt][:].rearrange("p (k c) -> p k c", k=KC),
                          w0T.rearrange("(k p) c -> p k c", p=128)
                          [:, :, dt * 128:(dt + 1) * 128])
        block.gpsimd(g_loads)

        def s_loads(sync):
            dma.start(sync, eb0_t[:], eb0)
        block.sync(s_loads)

        for dt in range(KC):
            def g_in(gp, dt=dt):
                chain.wait(gp)
                dma.start(gp, xch[:], x1T[dt * 128:(dt + 1) * 128, :])
            block.gpsimd(g_in)

            def t_mm(t, dt=dt):
                dma.wait_all(t)
                chain.wait(t)
                for h in range(2):
                    for k in range(KC):
                        t.matmul(ps[h][:, :],
                                 wob[dt % 2][:, k * 128:(k + 1) * 128],
                                 o1t[:, k * TSL + h * TG:k * TSL + (h + 1) * TG],
                                 start=(k == 0), stop=(k == KC - 1))
                    for k in range(KC):
                        t.matmul(ps[2 + h][:, :],
                                 w0b[dt % 2][:, k * 128:(k + 1) * 128],
                                 xn1t[:, k * TSL + h * TG:k * TSL + (h + 1) * TG],
                                 start=(k == 0), stop=(k == KC - 1))
                chain.bump_d(t)
                t.nop().then_inc(wfree, 1)
                wfree_n[0] += 1
            block.tensor(t_mm)

            if dt + 2 < KC:
                def g_wn(gp, dt=dt, tgt=wfree_n[0]):
                    gp.wait_ge(wfree, tgt)
                    dma.start(gp, wob[dt % 2][:].rearrange("p (k c) -> p k c", k=KC),
                              wo.rearrange("(k p) c -> p k c", p=128)
                              [:, :, (dt + 2) * 128:(dt + 3) * 128])
                    dma.start(gp, w0b[dt % 2][:].rearrange("p (k c) -> p k c", k=KC),
                              w0T.rearrange("(k p) c -> p k c", p=128)
                              [:, :, (dt + 2) * 128:(dt + 3) * 128])
                block.gpsimd(g_wn)

            def s_ev(s, dt=dt):
                chain.wait(s)
                for h in range(2):
                    s.copy(attc[:, h * TG:(h + 1) * TG], ps[h][:, :])
                chain.bump_d(s)
            block.scalar(s_ev)

            def v_fin(v, dt=dt):
                chain.wait(v)
                for h in range(2):
                    v.tensor_scalar(ffnc[:, h * TG:(h + 1) * TG],
                                    ps[2 + h][:, :], eb0_t[:, dt:dt + 1], None,
                                    ALU.add)
                v.drain()
                v.tensor_tensor(outc[:], xch[:], attc[:], ALU.add)
                v.drain()
                v.tensor_tensor(outc[:], outc[:], ffnc[:], ALU.add)
                chain.bump_d(v)
            block.vector(v_fin)

            def g_out(gp, dt=dt):
                chain.wait(gp)
                sto.start(gp, outTo[dt * 128:(dt + 1) * 128, :], outc[:])
                chain.bump_d(gp)
            block.gpsimd(g_out)

        def fin(sync):
            chain.wait(sync)
            sto.wait_all(sync)
        block.sync(fin)

    return nc


def _stage_l1_inputs(inputs, core, xT):
    h0 = core * HPC
    sl = slice(h0 * DH, (h0 + HPC) * DH)

    def wt(W):
        return np.ascontiguousarray(np.asarray(W, np.float32)[sl, :].T)

    d = {
        "xT": xT,
        "wqT": wt(inputs["L0_attn_Wq"]), "wkT": wt(inputs["L0_attn_Wk"]),
        "wvT": wt(inputs["L0_attn_Wv"]), "waT": wt(inputs["L0_attn_Wa"]),
        "wgT": wt(inputs["L0_attn_Wg"]),
        "wbT": np.ascontiguousarray(
            np.asarray(inputs["L0_attn_Wb"], np.float32)[h0:h0 + HPC, :].T),
        "ba": np.ascontiguousarray(
            np.asarray(inputs["L0_attn_ba"], np.float32)[sl].reshape(HPC, 128).T),
        "bb": np.ascontiguousarray(
            np.asarray(inputs["L0_attn_bb"], np.float32)[h0:h0 + HPC].reshape(HPC, 1)),
        "onescol_bf": np.ones((128, 1), dtype=BF),
        "onesrow_f": np.ones((1, 128), dtype=BF),
    }
    for p, wn, bn in (("q", "L0_attn_qconv_w", "L0_attn_qconv_b"),
                      ("k", "L0_attn_kconv_w", "L0_attn_kconv_b"),
                      ("v", "L0_attn_vconv_w", "L0_attn_vconv_b")):
        cwv = np.asarray(inputs[wn], np.float32)[sl, :]
        d[f"{p}cw"] = np.ascontiguousarray(
            cwv.reshape(HPC, 128, 4).transpose(1, 0, 2).reshape(128, HPC * 4))
        d[f"{p}cb"] = np.ascontiguousarray(
            np.asarray(inputs[bn], np.float32)[sl].reshape(HPC, 128).T)
    return d


_XT_CACHE = [None]


def run_launch1(inputs):
    x = np.asarray(inputs["L0_attn_x"], dtype=np.float32).reshape(NTOK, D)
    xT = np.ascontiguousarray(x.T)
    _XT_CACHE[0] = xT
    nc = build_launch1()
    in_maps = [_stage_l1_inputs(inputs, c, xT) for c in range(NCORES)]
    res = run_bass_kernel_spmd(nc, in_maps, core_ids=list(range(NCORES)))
    return res.results


_DBG = {}


def kernel(**inputs):
    x = np.asarray(inputs["L0_attn_x"], dtype=np.float32).reshape(NTOK, D)
    xT = np.ascontiguousarray(x.T)
    eye_bf = np.eye(128, dtype=BF)
    ones_col = np.ones((128, 1), dtype=BF)

    nc1 = build_launch1()
    im1 = [_stage_l1_inputs(inputs, c, xT) for c in range(NCORES)]
    r1 = _DBG['r1'] = run_bass_kernel_spmd(nc1, im1, core_ids=list(range(NCORES))).results

    nc2 = build_launch2()
    tri = np.triu(np.ones((C2C, C2C), dtype=np.float32)).astype(BF)
    lnw = np.asarray(inputs["L0_attn_ln_w"], np.float32).reshape(128, 1)
    lnb = np.asarray(inputs["L0_attn_ln_b"], np.float32).reshape(128, 1)
    im2 = []
    for c in range(NCORES):
        d = r1[c]
        im2.append({
            "qTi": np.asarray(d["qTo"]), "kTi": np.asarray(d["kTo"]),
            "vTi": np.asarray(d["vTo"]), "aTi": np.asarray(d["aTo"]),
            "gTi": np.asarray(d["gTo"]), "bTi": np.asarray(d["bTo"]),
            "lnw": lnw, "lnb": lnb, "ident": eye_bf,
            "onescol_bf": ones_col, "triu": tri,
        })
    r2 = _DBG['r2'] = run_bass_kernel_spmd(nc2, im2, core_ids=list(range(NCORES))).results

    nc3 = build_launch3()
    xnT_full = _DBG['xnT_full'] = np.asarray(r1[0]["xnTo"])
    gwf = np.ascontiguousarray(np.asarray(inputs["L0_ffn_gate_w"], np.float32))
    uwf = np.ascontiguousarray(np.asarray(inputs["L0_ffn_up_w"], np.float32))
    dw = np.asarray(inputs["L0_ffn_down_w"], np.float32)
    dwt = np.ascontiguousarray(
        dw.reshape(FE, FFT, 128, 4, 512).transpose(3, 0, 1, 2, 4))
    rwf = np.ascontiguousarray(np.asarray(inputs["L0_ffn_router_w"], np.float32))
    im3 = []
    for c in range(NCORES):
        im3.append({
            "xnT": np.ascontiguousarray(xnT_full[:, c * TSL:(c + 1) * TSL]),
            "rw": rwf, "gw": gwf, "uw": uwf, "dwt": dwt, "ident": eye_bf,
        })
    r3 = _DBG['r3'] = run_bass_kernel_spmd(nc3, im3, core_ids=list(range(NCORES))).results

    nc4 = build_launch4()
    oT_full = _DBG['oT_full'] = np.concatenate([np.asarray(r2[c]["oTo"]) for c in range(NCORES)],
                             axis=0)
    woT = np.ascontiguousarray(np.asarray(inputs["L0_attn_Wo"], np.float32).T)
    im4 = []
    for c in range(NCORES):
        cs = slice(c * TSL, (c + 1) * TSL)
        im4.append({
            "oT": np.ascontiguousarray(oT_full[:, cs]),
            "woT": woT,
            "xTs": np.ascontiguousarray(xT[:, cs]),
            "ffnT": np.asarray(r3[c]["ffnTo"]),
            "onescol_bf": ones_col,
        })
    r4 = _DBG['r4'] = run_bass_kernel_spmd(nc4, im4, core_ids=list(range(NCORES))).results

    nc5 = build_launch5()
    xn1_full = _DBG['xn1_full'] = np.concatenate([np.asarray(r4[c]["xn1To"])
                               for c in range(NCORES)], axis=1)
    wqkv = np.asarray(inputs["L1_attn_w_qkv"], np.float32)
    mk = np.zeros((4, 128, QW), dtype=np.float32)
    for m in range(4):
        for kl in range(128):
            mk[m, kl, :] = (np.arange(QW) >= (m * 128 + kl)).astype(np.float32)
    cmask = np.ascontiguousarray(mk.reshape(4 * 128, QW)).astype(BF)
    im5 = []
    for c in range(NCORES):
        h0 = c * HPC
        csl = slice(h0 * 128, (h0 + HPC) * 128)
        im5.append({
            "xn1": xn1_full,
            "wq": np.ascontiguousarray(wqkv[:, csl]),
            "wk": np.ascontiguousarray(wqkv[:, 2048:][:, csl]),
            "wv": np.ascontiguousarray(wqkv[:, 4096:][:, csl]),
            "ident": eye_bf, "onescol_bf": ones_col, "cmask": cmask,
        })
    r5 = _DBG['r5'] = run_bass_kernel_spmd(nc5, im5, core_ids=list(range(NCORES))).results

    nc6 = build_launch6()
    o1_full = _DBG['o1_full'] = np.concatenate([np.asarray(r5[c]["o1To"]) for c in range(NCORES)],
                             axis=0)
    wo6 = np.ascontiguousarray(np.asarray(inputs["L1_attn_w_out"], np.float32))
    w0T = np.ascontiguousarray(
        np.asarray(inputs["L1_ffn_expert_w"], np.float32)[0].T)
    eb0 = np.ascontiguousarray(
        np.asarray(inputs["L1_ffn_expert_b"], np.float32)[0].reshape(KC, 128).T)
    im6 = []
    for c in range(NCORES):
        cs = slice(c * TSL, (c + 1) * TSL)
        im6.append({
            "o1": np.ascontiguousarray(o1_full[:, cs]),
            "wo6": wo6, "w0T": w0T, "eb0": eb0,
            "xn1Ts": np.asarray(r4[c]["xn1To"]),
            "x1Ts": np.asarray(r4[c]["x1To"]),
        })
    r6 = _DBG['r6'] = run_bass_kernel_spmd(nc6, im6, core_ids=list(range(NCORES))).results

    outT = np.concatenate([np.asarray(r6[c]["outTo"], np.float32)
                           for c in range(NCORES)], axis=1)
    return np.ascontiguousarray(outT.T).reshape(B, T, D).astype(np.float32)
